# version 02
# baseline (speedup 1.0000x reference)
import numpy as np
import ml_dtypes
import concourse.bass as bass
import concourse.mybir as mybir
from concourse.bass_utils import run_bass_kernel_spmd

F32 = mybir.dt.float32
BF16 = mybir.dt.bfloat16
I32 = mybir.dt.int32
AF = mybir.ActivationFunctionType
ALU = mybir.AluOpType
BF = ml_dtypes.bfloat16

NCORES = 8
B, T, D = 4, 4096, 1024
NTOK = B * T
S = NTOK // NCORES
EPS = 1e-5


def ceil_div(a, b):
    return -(-a // b)

_UID = [0]


def _nm(base):
    _UID[0] += 1
    return f"{base}_{_UID[0]}"




def mm(nc, out, lhsT, rhs, start, stop, tile_position=None):
    n = rhs.shape[-1]
    kw = {} if tile_position is None else {"tile_position": tile_position}
    for j in range(ceil_div(n, 512)):
        cols = slice(j * 512, min((j + 1) * 512, n))
        nc.tensor.matmul(out[..., cols], lhsT, rhs[..., cols],
                         start=start, stop=stop, skip_group_check=True, **kw)


class Prog:

    def __init__(self, nc):
        self.nc = nc
        self.dma_sems = {}
        self.dma_counts = {}

    def dma(self, qeng, out, in_, **kw):
        key = qeng.engine.name
        if key not in self.dma_sems:
            self.dma_sems[key] = (qeng, self.nc.alloc_semaphore(f"dmasem_{key}"))
            self.dma_counts[key] = 0
        _, sem = self.dma_sems[key]
        qeng.dma_start(out=out, in_=in_, **kw).then_inc(sem, 16)
        self.dma_counts[key] += 16
        return sem

    def dma_wait_all(self):
        for key, (eng, sem) in self.dma_sems.items():
            eng.wait_ge(sem, self.dma_counts[key])

    def barrier(self):
        self.dma_wait_all()
        self.nc.all_engine_barrier()


def emit_ln_fm(p, x_f32, xb_out, sq_buf, row_r, row_mr, t_cols,
               nchunk=8, x_src_chunks=None):
    nc = p.nc
    dmodel = 128 * nchunk
    with (
        nc.sbuf_tensor(_nm("ln_ones"), [128, 1], BF16) as ones,
        nc.sbuf_tensor(_nm("ln_m"), [1, t_cols], F32) as row_m,
        nc.sbuf_tensor(_nm("ln_v"), [1, t_cols], F32) as row_v,
        nc.psum_tensor(_nm("ln_ps"), [1, t_cols], F32) as ps_s,
        nc.psum_tensor(_nm("ln_ps2"), [1, t_cols], F32) as ps_q,
    ):
        nc.vector.memset(ones[:, :], 1.0)
        if x_src_chunks is None:
            for c in range(nchunk):
                nc.vector.tensor_copy(out=xb_out[:, c], in_=x_f32[:, c])
        p.barrier()
        ncol = ceil_div(t_cols, 512)
        for c in range(nchunk):
            nc.scalar.activation(out=sq_buf[:, :], in_=xb_out[:, c], func=AF.Square)
            p.barrier()
            for j in range(ncol):
                cols = slice(j * 512, min((j + 1) * 512, t_cols))
                nc.tensor.matmul(ps_s[0:1, cols], ones[:, :], xb_out[:, c, cols],
                                 start=(c == 0), stop=(c == nchunk - 1),
                                 skip_group_check=True)
                nc.tensor.matmul(ps_q[0:1, cols], ones[:, :], sq_buf[:, cols],
                                 start=(c == 0), stop=(c == nchunk - 1),
                                 skip_group_check=True)
            p.barrier()
        m = row_m[0:1, :]
        var = row_v[0:1, :]
        nc.vector.tensor_scalar_mul(m, ps_s[0:1, :], 1.0 / dmodel)
        p.barrier()
        nc.vector.tensor_mul(var, m, m)
        p.barrier()
        nc.vector.scalar_tensor_tensor(out=var, in0=ps_q[0:1, :], scalar=1.0 / dmodel,
                                       in1=var, op0=ALU.mult, op1=ALU.subtract)
        p.barrier()
        nc.vector.tensor_scalar_add(var, var, EPS)
        p.barrier()
        nc.scalar.activation(out=var, in_=var, func=AF.Sqrt)
        p.barrier()
        nc.vector.reciprocal(row_r[0:1, :], var)
        p.barrier()
        nc.vector.tensor_mul(row_mr[0:1, :], m, row_r[0:1, :])
        p.barrier()


def emit_bcast_row(p, row_f32, ps_out, t_cols, row_bf):
    nc = p.nc
    with nc.sbuf_tensor(_nm("bc_ones"), [1, 128], BF16) as ones1:
        nc.vector.memset(ones1[:, :], 1.0)
        nc.vector.tensor_copy(out=row_bf[0:1, 0:t_cols], in_=row_f32)
        p.barrier()
        for j in range(ceil_div(t_cols, 512)):
            cols = slice(j * 512, min((j + 1) * 512, t_cols))
            nc.tensor.matmul(ps_out[:, cols], ones1[:, :], row_bf[0:1, cols],
                             start=True, stop=True)
        p.barrier()


def emit_normalize_fm(p, x_bf, out_bf, r_ps, mr_ps, nchunk, t_cols,
                      gamma=None, beta=None):
    nc = p.nc
    for c in range(nchunk):
        nc.vector.tensor_mul(out_bf[:, c], x_bf[:, c], r_ps[:, 0:t_cols])
    p.barrier()
    for c in range(nchunk):
        nc.vector.tensor_sub(out_bf[:, c], out_bf[:, c], mr_ps[:, 0:t_cols])
    p.barrier()
    if gamma is not None:
        for c in range(nchunk):
            nc.vector.tensor_scalar(out=out_bf[:, c], in0=out_bf[:, c],
                                    scalar1=gamma[:, c], scalar2=beta[:, c],
                                    op0=ALU.mult, op1=ALU.add)
        p.barrier()


def emit_sandwich_out(p, y_bf, x_f32_chunks, out_dram_chunks, r_ps, mr_ps,
                      nchunk, t_cols, qeng):
    nc = p.nc
    with nc.sbuf_tensor(_nm("swo"), [128, t_cols], F32) as tmp:
        for c in range(nchunk):
            nc.vector.tensor_mul(tmp[:, :], y_bf[:, c], r_ps[:, 0:t_cols])
            nc.vector.tensor_sub(tmp[:, :], tmp[:, :], mr_ps[:, 0:t_cols])
            nc.vector.tensor_add(tmp[:, :], tmp[:, :], x_f32_chunks[c])
            p.barrier()
            p.dma(qeng, out_dram_chunks[c], tmp[:, :])
            p.barrier()


def build_l1ffn():
    HID = 3072
    NH = HID // 128
    SC = 1.0 / 127.0
    SH = S // 2
    nc = bass.Bass("TRN2", target_bir_lowering=False)
    xin = nc.dram_tensor("xin", [8, 128, S], F32, kind="ExternalInput").ap()
    wgT = nc.dram_tensor("wgT", [8, 128, HID], I32, kind="ExternalInput").ap()
    wuT = nc.dram_tensor("wuT", [8, 128, HID], I32, kind="ExternalInput").ap()
    wdT = nc.dram_tensor("wdT", [NH, 128, D], I32, kind="ExternalInput").ap()
    xout = nc.dram_tensor("xout", [8, 128, S], F32, kind="ExternalOutput").ap()

    p = Prog(nc)
    with (
        nc.sbuf_tensor("xp", [128, 8, S], BF16) as xp,
        nc.sbuf_tensor("ybuf", [128, 8, S], BF16) as ybuf,
        nc.sbuf_tensor("wd", [128, NH, D], BF16) as wd,
        nc.sbuf_tensor("stg", [128, 8, 128], I32) as stg,
        nc.sbuf_tensor("sq", [128, S], BF16) as sq,
        nc.sbuf_tensor("rowr", [1, S], F32) as rowr,
        nc.sbuf_tensor("rowmr", [1, S], F32) as rowmr,
        nc.sbuf_tensor("rowbf", [1, S], BF16) as rowbf,
        nc.sbuf_tensor("xst", [128, S], F32) as xst,
        nc.sbuf_tensor("whc", [128, 2, 8, 128], BF16) as whc,
        nc.sbuf_tensor("gbuf", [128, SH], BF16) as gbuf,
    ):
        with nc.sbuf_tensor("xb", [128, 8, S], BF16) as xb:
            for c in range(8):
                p.dma(nc.sync, xst[:, :], xin[c])
                p.barrier()
                nc.vector.tensor_copy(out=xb[:, c], in_=xst[:, :])
                p.barrier()
            emit_ln_fm(p, None, xb, sq, rowr[0:1], rowmr[0:1], S,
                       x_src_chunks=[xb[:, c] for c in range(8)])
            with (
                nc.psum_tensor("rps", [128, 512], F32) as rps,
                nc.psum_tensor("mrps", [128, 512], F32) as mrps,
            ):
                for tt in range(S // 512):
                    cols = slice(tt * 512, (tt + 1) * 512)
                    emit_bcast_row(p, rowr[0:1, cols], rps, 512, rowbf)
                    emit_bcast_row(p, rowmr[0:1, cols], mrps, 512, rowbf)
                    for c in range(8):
                        nc.vector.tensor_mul(xp[:, c, cols], xb[:, c, cols], rps[:, :])
                    p.barrier()
                    for c in range(8):
                        nc.vector.tensor_sub(xp[:, c, cols], xp[:, c, cols], mrps[:, :])
                    p.barrier()

        with nc.sbuf_tensor("hbuf", [128, NH, SH], BF16) as hbuf:
            for hc in range(NH):
                p.dma(nc.sync, stg.ap().rearrange("p a b -> p (a b)")[:, 0:D], wdT[hc])
                p.barrier()
                nc.vector.tensor_copy(out=wd[:, hc],
                                      in_=stg.ap().rearrange("p a b -> p (a b)")[:, 0:D])
                p.barrier()

            for half in range(2):
                cols = slice(half * SH, (half + 1) * SH)
                with (
                    nc.psum_tensor(_nm("gps"), [128, SH], F32) as gps,
                    nc.psum_tensor(_nm("ups"), [128, SH], F32) as ups,
                ):
                    for hc in range(NH):
                        hs = slice(hc * 128, (hc + 1) * 128)
                        p.dma(nc.sync, stg[:, :, :], wgT[:, :, hs].rearrange("a p b -> p a b"))
                        p.barrier()
                        nc.vector.tensor_copy(out=whc[:, 0], in_=stg[:, :, :])
                        p.barrier()
                        p.dma(nc.sync, stg[:, :, :], wuT[:, :, hs].rearrange("a p b -> p a b"))
                        p.barrier()
                        nc.vector.tensor_copy(out=whc[:, 1], in_=stg[:, :, :])
                        p.barrier()
                        for dc in range(8):
                            mm(nc, gps[:, :], whc[:, 0, dc], xp[:, dc, cols],
                               start=(dc == 0), stop=(dc == 7))
                            mm(nc, ups[:, :], whc[:, 1, dc], xp[:, dc, cols],
                               start=(dc == 0), stop=(dc == 7))
                        p.barrier()
                        nc.scalar.activation(out=gbuf[:, :], in_=gps[:, :],
                                             func=AF.Silu, scale=SC)
                        nc.scalar.activation(out=hbuf[:, hc], in_=ups[:, :],
                                             func=AF.Copy, scale=SC * SC)
                        p.barrier()
                        nc.vector.tensor_mul(hbuf[:, hc], gbuf[:, :], hbuf[:, hc])
                    p.barrier()
                for dq in range(2):
                    with nc.psum_tensor(_nm("dps"), [128, 4, SH], F32) as dps:
                        for i in range(4):
                            dc = dq * 4 + i
                            for hc in range(NH):
                                mm(nc, dps[:, i], wd[:, hc, dc * 128:(dc + 1) * 128],
                                   hbuf[:, hc], start=(hc == 0), stop=(hc == NH - 1))
                        p.barrier()
                        for i in range(4):
                            nc.vector.tensor_copy(out=ybuf[:, dq * 4 + i, cols],
                                                  in_=dps[:, i])
                        p.barrier()

        emit_ln_fm(p, None, ybuf, sq, rowr[0:1], rowmr[0:1], S,
                   x_src_chunks=[ybuf[:, c] for c in range(8)])
        with (
            nc.psum_tensor("rps2", [128, SH], F32) as rps2,
            nc.psum_tensor("mrps2", [128, SH], F32) as mrps2,
        ):
            for half in range(2):
                cols = slice(half * SH, (half + 1) * SH)
                emit_bcast_row(p, rowr[0:1, cols], rps2, SH, rowbf)
                emit_bcast_row(p, rowmr[0:1, cols], mrps2, SH, rowbf)
                with nc.sbuf_tensor(_nm("swo"), [128, SH], F32) as tmp:
                    for c in range(8):
                        p.dma(nc.sync, xst[:, 0:SH],
                              xin[c].rearrange("p (h t) -> p h t", h=2)[:, half])
                        p.barrier()
                        nc.vector.tensor_mul(tmp[:, :], ybuf[:, c, cols], rps2[:, :])
                        p.barrier()
                        nc.vector.tensor_sub(tmp[:, :], tmp[:, :], mrps2[:, :])
                        p.barrier()
                        nc.vector.tensor_add(tmp[:, :], tmp[:, :], xst[:, 0:SH])
                        p.barrier()
                        p.dma(nc.sync,
                              xout[c].rearrange("p (h t) -> p h t", h=2)[:, half],
                              tmp[:, :])
                        p.barrier()
        p.barrier()

    return nc


def prep_l1ffn_inputs(x3_fm, w_gate, w_up, w_down):
    wgT = np.ascontiguousarray(w_gate.T).reshape(8, 128, 3072)
    wuT = np.ascontiguousarray(w_up.T).reshape(8, 128, 3072)
    wdT = np.ascontiguousarray(w_down.T).reshape(24, 128, 1024)
    maps = []
    for c in range(NCORES):
        xin = np.ascontiguousarray(
            x3_fm[:, c * S:(c + 1) * S]).reshape(8, 128, S)
        maps.append({"xin": xin, "wgT": wgT, "wuT": wuT, "wdT": wdT})
    return maps


def run_l1ffn(x3_fm, w_gate, w_up, w_down):
    nc = build_l1ffn()
    maps = prep_l1ffn_inputs(x3_fm, w_gate, w_up, w_down)
    res = run_bass_kernel_spmd(nc, maps, core_ids=list(range(NCORES)))
    out = np.concatenate([r["xout"].reshape(1024, S) for r in res.results], axis=1)
    return out




def build_scan(T_steps=T):
    nc = bass.Bass("TRN2", target_bir_lowering=False)
    xw = nc.dram_tensor("xw", [T_steps, 4096], BF16, kind="ExternalInput").ap()
    xa = nc.dram_tensor("xa", [T_steps, 64], BF16, kind="ExternalInput").ap()
    u0 = nc.dram_tensor("u0", [8, 128, 2048], F32, kind="ExternalInput").ap()
    u1 = nc.dram_tensor("u1", [8, 128, 2112], F32, kind="ExternalInput").ap()
    pT = nc.dram_tensor("pT", [64, 1024], F32, kind="ExternalInput").ap()
    alpha = nc.dram_tensor("alpha", [1, 1024], F32, kind="ExternalInput").ap()
    sh = nc.dram_tensor("sh", [T_steps, 1024], BF16, kind="ExternalOutput").ap()

    p = Prog(nc)
    import contextlib
    es = contextlib.ExitStack()
    with es:
        u0b = es.enter_context(nc.sbuf_tensor("u0b", [128, 8, 2048], BF16))
        u1b = es.enter_context(nc.sbuf_tensor("u1b", [128, 8, 2112], BF16))
        stgf = es.enter_context(nc.sbuf_tensor("stgf", [128, 2112], F32))
        pTb = es.enter_context(nc.sbuf_tensor("pTb", [64, 1024], BF16))
        alpr = es.enter_context(nc.sbuf_tensor("alpr", [1, 1024], F32))
        xwrow = es.enter_context(nc.sbuf_tensor("xwrow", [1, 4096], BF16))
        xarow = es.enter_context(nc.sbuf_tensor("xarow", [1, 64], BF16))
        one1 = es.enter_context(nc.sbuf_tensor("one1", [1, 1], BF16))
        ident = es.enter_context(nc.sbuf_tensor("ident", [1, 1], BF16))
        hT = es.enter_context(nc.sbuf_tensor("hT", [128, 8], BF16))
        hb = es.enter_context(nc.sbuf_tensor("hb", [1, 1024], BF16))
        cst = es.enter_context(nc.sbuf_tensor("cst", [1, 1024], F32))
        sif = es.enter_context(nc.sbuf_tensor("sif", [1, 2048], F32))
        sog = es.enter_context(nc.sbuf_tensor("sog", [1, 2048], F32))
        rbb = es.enter_context(nc.sbuf_tensor("rbb", [1, 64], BF16))
        rbT = es.enter_context(nc.sbuf_tensor("rbT", [64, 1], BF16))
        tm1 = es.enter_context(nc.sbuf_tensor("tm1", [1, 1024], F32))
        tm2 = es.enter_context(nc.sbuf_tensor("tm2", [1, 1024], F32))
        tc = es.enter_context(nc.sbuf_tensor("tc", [1, 1024], F32))
        Zp = es.enter_context(nc.psum_tensor("Z", [64, 2112], F32))
        psM = es.enter_context(nc.psum_tensor("psM", [1, 1024], F32))
        pst = es.enter_context(nc.psum_tensor("pst", [128, 16, 2], BF16))
        ld = es.enter_context(nc.semaphore("ld"))
        st = es.enter_context(nc.semaphore("st"))
        nc.vector.memset(one1[:, :], 1.0)
        nc.vector.memset(ident[:, :], 1.0)
        nc.vector.memset(hT[:, :], 0.0)
        nc.vector.memset(cst[:, :], 0.0)
        p.dma(nc.sync, stgf[0:64, 0:1024], pT[:, :])
        p.barrier()
        nc.vector.tensor_copy(out=pTb[:, :], in_=stgf[0:64, 0:1024])
        p.barrier()
        p.dma(nc.sync, alpr[0:1, :], alpha[:, :])
        p.barrier()
        for k in range(8):
            p.dma(nc.sync, stgf[:, 0:2048], u0[k])
            p.barrier()
            nc.vector.tensor_copy(out=u0b[:, k], in_=stgf[:, 0:2048])
            p.barrier()
            p.dma(nc.sync, stgf[:, 0:2112], u1[k])
            p.barrier()
            nc.vector.tensor_copy(out=u1b[:, k], in_=stgf[:, 0:2112])
            p.barrier()
        p.barrier()

        bar = nc.all_engine_barrier
        with nc.Fori(0, T_steps) as i:
            nc.sync.dma_start(out=xwrow[0:1, :], in_=xw[i]).then_inc(ld, 16)
            nc.sync.dma_start(out=xarow[0:1, :], in_=xa[i]).then_inc(ld, 16)
            nc.sync.wait_ge(ld, (i + 1) * 32)
            bar()
            for k in range(8):
                mm(nc, Zp[0:1, 0:2048], hT[:, k:k + 1], u0b[:, k],
                   start=(k == 0), stop=False, tile_position=(0, 0))
                mm(nc, Zp[32:33, 0:2112], hT[:, k:k + 1], u1b[:, k],
                   start=(k == 0), stop=False, tile_position=(0, 32))
            mm(nc, Zp[0:1, 0:2048], one1[0:1, :], xwrow[0:1, 0:2048],
               start=False, stop=True, tile_position=(0, 0))
            mm(nc, Zp[32:33, 0:2048], one1[0:1, :], xwrow[0:1, 2048:4096],
               start=False, stop=True, tile_position=(0, 32))
            bar()
            nc.scalar.activation(out=sif[0:1, :], in_=Zp[0:1, 0:2048], func=AF.Sigmoid)
            nc.scalar.activation(out=sog[0:1, 0:1024], in_=Zp[32:33, 0:1024],
                                 func=AF.Sigmoid)
            nc.scalar.activation(out=sog[0:1, 1024:2048], in_=Zp[32:33, 1024:2048],
                                 func=AF.Tanh)
            nc.vector.tensor_mul(rbb[0:1, :], xarow[0:1, :], Zp[32:33, 2048:2112])
            bar()
            nc.tensor.transpose(pst[0:64, 15, 0:1], rbb[0:1, :], ident[0:1, 0:1])
            bar()
            nc.vector.tensor_copy(out=rbT[:, 0:1], in_=pst[0:64, 15, 0:1])
            bar()
            mm(nc, psM[0:1, :], rbT[:, 0:1], pTb[:, :], start=True, stop=True)
            nc.vector.tensor_mul(tm1[0:1, :], sif[0:1, 0:1024], sog[0:1, 1024:2048])
            bar()
            nc.vector.tensor_mul(tm2[0:1, :], sif[0:1, 1024:2048], cst[0:1, :])
            bar()
            nc.vector.tensor_add(tm1[0:1, :], tm1[0:1, :], tm2[0:1, :])
            bar()
            nc.vector.tensor_mul(tm1[0:1, :], tm1[0:1, :], alpr[0:1, :])
            bar()
            nc.vector.scalar_tensor_tensor(out=cst[0:1, :], in0=psM[0:1, :],
                                           scalar=0.1, in1=tm1[0:1, :],
                                           op0=ALU.mult, op1=ALU.add)
            bar()
            nc.scalar.activation(out=tc[0:1, :], in_=cst[0:1, :], func=AF.Tanh)
            bar()
            nc.vector.tensor_mul(hb[0:1, :], sog[0:1, 0:1024], tc[0:1, :])
            bar()
            for k in range(8):
                nc.tensor.transpose(pst[:, k, 0:1], hb[0:1, k * 128:(k + 1) * 128],
                                    ident[0:1, 0:1])
            bar()
            nc.vector.tensor_copy(out=hT[:, :], in_=pst[:, 0:8, 0])
            nc.gpsimd.dma_start(out=sh[i], in_=hb[0:1, :]).then_inc(st, 16)
            nc.gpsimd.wait_ge(st, (i + 1) * 16)
            bar()
    return nc


def prep_scan_inputs(U, Bm, P, alpha, xw_bf, xa_bf):
    UT = np.ascontiguousarray(U.T)
    u0 = np.ascontiguousarray(UT.reshape(8, 128, 4096)[:, :, 0:2048])
    u1z = UT.reshape(8, 128, 4096)[:, :, 2048:4096]
    BmT = np.ascontiguousarray(Bm.T).reshape(8, 128, 64)
    u1 = np.ascontiguousarray(np.concatenate([u1z, BmT], axis=2))
    pTh = np.ascontiguousarray(P.T)
    al = np.ascontiguousarray(alpha.reshape(1, 1024))
    maps = []
    for c in range(NCORES):
        b = c % B
        maps.append({"xw": np.ascontiguousarray(xw_bf[b]),
                     "xa": np.ascontiguousarray(xa_bf[b]),
                     "u0": u0, "u1": u1, "pT": pTh, "alpha": al})
    return maps


def build_premid(do_ln, zw):
    nc = bass.Bass("TRN2", target_bir_lowering=False)
    xin = nc.dram_tensor("xin", [8, 128, S], F32, kind="ExternalInput").ap()
    wT = nc.dram_tensor("wT", [8, 128, zw], F32, kind="ExternalInput").ap()
    b0 = nc.dram_tensor("b0", [1, 4096], F32, kind="ExternalInput").ap()
    b1 = nc.dram_tensor("b1", [1, 4096], F32, kind="ExternalInput").ap()
    xwout = nc.dram_tensor("xwout", [S, zw], BF16, kind="ExternalOutput").ap()
    p = Prog(nc)
    import contextlib
    es = contextlib.ExitStack()
    with es:
        xp = es.enter_context(nc.sbuf_tensor("xp", [128, 8, S], BF16))
        wTb = es.enter_context(nc.sbuf_tensor("wTb", [128, 8, zw], BF16))
        stgf = es.enter_context(nc.sbuf_tensor("stgf", [128, zw], F32))
        brow = es.enter_context(nc.sbuf_tensor("brow", [1, 4096], F32))
        brf = es.enter_context(nc.sbuf_tensor("brf", [1, 4096], BF16))
        onec = es.enter_context(nc.sbuf_tensor("onec", [1, 128], BF16))
        xwt = es.enter_context(nc.sbuf_tensor("xwt", [128, zw], BF16))
        nc.vector.memset(onec[:, :], 1.0)
        p.dma(nc.sync, brow[0:1, :], b0[:, :])
        p.dma(nc.sync, stgf[0:1, 0:4096], b1[:, :])
        p.barrier()
        nc.vector.tensor_add(brow[0:1, :], brow[0:1, :], stgf[0:1, 0:4096])
        p.barrier()
        nc.vector.tensor_copy(out=brf[0:1, :], in_=brow[0:1, :])
        p.barrier()
        for dc in range(8):
            p.dma(nc.sync, stgf[:, :], wT[dc])
            p.barrier()
            nc.vector.tensor_copy(out=wTb[:, dc], in_=stgf[:, :])
            p.barrier()
        if do_ln:
            with (
                nc.sbuf_tensor("sq", [128, S], BF16) as sq,
                nc.sbuf_tensor("rowr", [1, S], F32) as rowr,
                nc.sbuf_tensor("rowmr", [1, S], F32) as rowmr,
                nc.sbuf_tensor("rowbf", [1, S], BF16) as rowbf,
            ):
                for c in range(8):
                    p.dma(nc.sync, stgf[:, 0:S], xin[c])
                    p.barrier()
                    nc.vector.tensor_copy(out=xp[:, c], in_=stgf[:, 0:S])
                    p.barrier()
                emit_ln_fm(p, None, xp, sq, rowr[0:1], rowmr[0:1], S,
                           x_src_chunks=[xp[:, c] for c in range(8)])
                with (
                    nc.psum_tensor("rps", [128, 512], F32) as rps,
                    nc.psum_tensor("mrps", [128, 512], F32) as mrps,
                ):
                    for tt in range(S // 512):
                        cols = slice(tt * 512, (tt + 1) * 512)
                        emit_bcast_row(p, rowr[0:1, cols], rps, 512, rowbf)
                        emit_bcast_row(p, rowmr[0:1, cols], mrps, 512, rowbf)
                        for c in range(8):
                            nc.vector.tensor_mul(xp[:, c, cols], xp[:, c, cols], rps[:, :])
                        p.barrier()
                        for c in range(8):
                            nc.vector.tensor_sub(xp[:, c, cols], xp[:, c, cols], mrps[:, :])
                        p.barrier()
        else:
            for c in range(8):
                p.dma(nc.sync, stgf[:, 0:S], xin[c])
                p.barrier()
                nc.vector.tensor_copy(out=xp[:, c], in_=stgf[:, 0:S])
                p.barrier()
        for tt in range(S // 128):
            ts = slice(tt * 128, (tt + 1) * 128)
            with nc.psum_tensor(_nm("gps"), [128, 8, 512], F32) as ps:
                for zc in range(8):
                    zs = slice(zc * 512, (zc + 1) * 512)
                    for dc in range(8):
                        nc.tensor.matmul(ps[:, zc], xp[:, dc, ts], wTb[:, dc, zs],
                                         start=(dc == 0), stop=False,
                                         skip_group_check=True)
                    nc.tensor.matmul(ps[:, zc], onec[0:1, :], brf[0:1, zs],
                                     start=False, stop=True,
                                     skip_group_check=True)
                p.barrier()
                for zc in range(8):
                    nc.vector.tensor_copy(out=xwt[:, zc * 512:(zc + 1) * 512],
                                          in_=ps[:, zc])
                p.barrier()
            if zw > 4096:
                with nc.psum_tensor(_nm("xps"), [128, zw - 4096], F32) as ps2:
                    for dc in range(8):
                        nc.tensor.matmul(ps2[:, :], xp[:, dc, ts], wTb[:, dc, 4096:zw],
                                         start=(dc == 0), stop=(dc == 7),
                                         skip_group_check=True)
                    p.barrier()
                    nc.vector.tensor_copy(out=xwt[:, 4096:zw], in_=ps2[:, :])
                    p.barrier()
            p.dma(nc.sync, xwout[ts], xwt[:, :])
            p.barrier()
    return nc


def build_post():
    nc = bass.Bass("TRN2", target_bir_lowering=False)
    xin = nc.dram_tensor("xin", [8, 128, S], F32, kind="ExternalInput").ap()
    hm = nc.dram_tensor("hm", [S, 1024], BF16, kind="ExternalInput").ap()
    ident = nc.dram_tensor("ident", [128, 128], F32, kind="ExternalInput").ap()
    xout = nc.dram_tensor("xout", [8, 128, S], F32, kind="ExternalOutput").ap()
    p = Prog(nc)
    import contextlib
    es = contextlib.ExitStack()
    with es:
        ymf = es.enter_context(nc.sbuf_tensor("ymf", [128, 8, S], BF16))
        idb = es.enter_context(nc.sbuf_tensor("idb", [128, 128], BF16))
        stg = es.enter_context(nc.sbuf_tensor("stg", [128, 1024], BF16))
        stf = es.enter_context(nc.sbuf_tensor("stf", [128, 1024], F32))
        sq = es.enter_context(nc.sbuf_tensor("sq", [128, S], BF16))
        rowr = es.enter_context(nc.sbuf_tensor("rowr", [1, S], F32))
        rowmr = es.enter_context(nc.sbuf_tensor("rowmr", [1, S], F32))
        rowbf = es.enter_context(nc.sbuf_tensor("rowbf", [1, S], BF16))
        xst = es.enter_context(nc.sbuf_tensor("xst", [128, S // 2], F32))
        p.dma(nc.sync, stf[:, 0:128], ident[:, :])
        p.barrier()
        nc.vector.tensor_copy(out=idb[:, :], in_=stf[:, 0:128])
        p.barrier()
        for tt in range(S // 128):
            p.dma(nc.sync, stg[:, :], hm[tt * 128:(tt + 1) * 128, :])
            p.barrier()
            with nc.psum_tensor(_nm("tps"), [128, 8, 128], BF16) as tps:
                for dc in range(8):
                    nc.tensor.transpose(tps[:, dc], stg[:, dc * 128:(dc + 1) * 128],
                                        idb[:, :])
                p.barrier()
                for dc in range(8):
                    nc.vector.tensor_copy(out=ymf[:, dc, tt * 128:(tt + 1) * 128],
                                          in_=tps[:, dc])
                p.barrier()
        emit_ln_fm(p, None, ymf, sq, rowr[0:1], rowmr[0:1], S,
                   x_src_chunks=[ymf[:, c] for c in range(8)])
        with (
            nc.psum_tensor("rps2", [128, S // 2], F32) as rps2,
            nc.psum_tensor("mrps2", [128, S // 2], F32) as mrps2,
        ):
            for half in range(2):
                cols = slice(half * (S // 2), (half + 1) * (S // 2))
                emit_bcast_row(p, rowr[0:1, cols], rps2, S // 2, rowbf)
                emit_bcast_row(p, rowmr[0:1, cols], mrps2, S // 2, rowbf)
                with nc.sbuf_tensor(_nm("swo"), [128, S // 2], F32) as tmp:
                    for c in range(8):
                        p.dma(nc.sync, xst[:, :],
                              xin[c].rearrange("p (h t) -> p h t", h=2)[:, half])
                        p.barrier()
                        nc.vector.tensor_mul(tmp[:, :], ymf[:, c, cols], rps2[:, :])
                        p.barrier()
                        nc.vector.tensor_sub(tmp[:, :], tmp[:, :], mrps2[:, :])
                        p.barrier()
                        nc.vector.tensor_add(tmp[:, :], tmp[:, :], xst[:, :])
                        p.barrier()
                        p.dma(nc.sync,
                              xout[c].rearrange("p (h t) -> p h t", h=2)[:, half],
                              tmp[:, :])
                        p.barrier()
    return nc



def _dump_qk(nc, p, yp, qT, kT, vh=None):
    with nc.sbuf_tensor(_nm("dmp"), [128, 512], F32) as d:
        for i, buf in ((0, qT), (1, kT)):
            for oc in range(4):
                for tt in range(T // 512):
                    nc.vector.tensor_copy(out=d[:, :],
                                          in_=buf[:, oc, tt * 512:(tt + 1) * 512])
                    p.barrier()
                    p.dma(nc.sync, yp[i * 4 + oc, :, tt * 512:(tt + 1) * 512], d[:, :])
                    p.barrier()

def build_l0attn(phases=6):
    NT = T // 128
    nc = bass.Bass("TRN2", target_bir_lowering=False)
    xin = nc.dram_tensor("xin", [8, 128, T], F32, kind="ExternalInput").ap()
    wqp = nc.dram_tensor("wqp", [8, 128, 512], F32, kind="ExternalInput").ap()
    wdn = nc.dram_tensor("wdn", [8, 128, 512], F32, kind="ExternalInput").ap()
    wkup = nc.dram_tensor("wkup", [4, 128, 512], F32, kind="ExternalInput").ap()
    wvup = nc.dram_tensor("wvup", [4, 128, 512], F32, kind="ExternalInput").ap()
    wop = nc.dram_tensor("wop", [4, 128, D], F32, kind="ExternalInput").ap()
    rc2 = nc.dram_tensor("rc2", [128, T], F32, kind="ExternalInput").ap()
    rs2 = nc.dram_tensor("rs2", [128, T], F32, kind="ExternalInput").ap()
    yp = nc.dram_tensor("yp", [8, 128, T], F32, kind="ExternalOutput").ap()

    p = Prog(nc)
    import contextlib
    es = contextlib.ExitStack()
    with es:
        qT = es.enter_context(nc.sbuf_tensor("qT", [128, 4, T], BF16))
        kT = es.enter_context(nc.sbuf_tensor("kT", [128, 4, T], BF16))
        vh = es.enter_context(nc.sbuf_tensor("vh", [128, NT, 8, 65], BF16))
        OT = es.enter_context(nc.sbuf_tensor("OT", [128, 4, T], BF16))
        nb = es.enter_context(nc.sbuf_tensor("nb", [128, 1], BF16))
        onec = es.enter_context(nc.sbuf_tensor("onec", [1, 128], BF16))
        onek = es.enter_context(nc.sbuf_tensor("onek", [128, 1], BF16))
        nc.vector.memset(onec[:, :], 1.0)
        nc.vector.memset(onek[:, :], 1.0)
        nc.vector.memset(vh.ap().rearrange("p a b c -> p (a b c)")[:, :], 0.0)
        nc.vector.memset(
            vh.ap().rearrange("p a b c -> p (a b) c", c=65)[:, :, 64:65], 1.0)
        p.barrier()

        with nc.sbuf_tensor("latT", [128, 4, T], BF16) as latT:
            with (
                nc.sbuf_tensor("wqb", [128, 8, 512], BF16) as wqb,
                nc.sbuf_tensor("wdnb", [128, 8, 512], BF16) as wdnb,
                nc.sbuf_tensor("stgf", [128, 2048], F32) as stgf,
                nc.sbuf_tensor("sqb", [128, 512], BF16) as sqb,
                nc.sbuf_tensor("rowr", [1, 512], F32) as rowr,
                nc.sbuf_tensor("rowmr", [1, 512], F32) as rowmr,
                nc.sbuf_tensor("rowbf", [1, 512], BF16) as rowbf,
            ):
                for wsrc, wdst in ((wqp, wqb), (wdn, wdnb)):
                    for c2 in range(2):
                        p.dma(nc.sync,
                              stgf.ap().rearrange("p (a b) -> p a b", a=4),
                              wsrc[c2 * 4:(c2 + 1) * 4].rearrange("a p b -> p a b"))
                        p.barrier()
                        nc.vector.tensor_copy(
                            out=wdst[:, c2 * 4:(c2 + 1) * 4],
                            in_=stgf.ap().rearrange("p (a b) -> p a b", a=4))
                        p.barrier()
                for tq8 in range(8):
                    tcols = slice(tq8 * 512, (tq8 + 1) * 512)
                    with nc.sbuf_tensor(_nm("xpq"), [128, 8, 512], BF16) as xp:
                        for c in range(8):
                            p.dma(nc.sync, stgf[:, 0:512], xin[c, :, tcols])
                            p.barrier()
                            nc.vector.tensor_copy(out=xp[:, c], in_=stgf[:, 0:512])
                            p.barrier()
                        emit_ln_fm(p, None, xp, sqb, rowr[0:1], rowmr[0:1], 512,
                                   x_src_chunks=[xp[:, c] for c in range(8)])
                        with (
                            nc.psum_tensor(_nm("rps"), [128, 512], F32) as rps,
                            nc.psum_tensor(_nm("mrps"), [128, 512], F32) as mrps,
                        ):
                            emit_bcast_row(p, rowr[0:1, 0:512], rps, 512, rowbf)
                            emit_bcast_row(p, rowmr[0:1, 0:512], mrps, 512, rowbf)
                            for c in range(8):
                                nc.vector.tensor_mul(xp[:, c], xp[:, c], rps[:, :])
                            p.barrier()
                            for c in range(8):
                                nc.vector.tensor_sub(xp[:, c], xp[:, c], mrps[:, :])
                            p.barrier()
                        for wdst, dst in ((wqb, qT), (wdnb, latT)):
                            with nc.psum_tensor(_nm("qps"), [128, 4, 512], F32) as ps:
                                for oc in range(4):
                                    for c in range(8):
                                        nc.tensor.matmul(
                                            ps[:, oc],
                                            wdst[:, c, oc * 128:(oc + 1) * 128],
                                            xp[:, c],
                                            start=(c == 0), stop=(c == 7),
                                            skip_group_check=True)
                                p.barrier()
                                for oc in range(4):
                                    nc.vector.tensor_copy(
                                        out=dst[:, oc, tcols], in_=ps[:, oc])
                                p.barrier()
            if phases < 2:
                _dump_qk(nc, p, yp, qT, kT)
                return nc
            with (
                nc.sbuf_tensor("wkb", [128, 4, 512], BF16) as wkb,
                nc.sbuf_tensor("wvb", [128, 4, 512], BF16) as wvb,
                nc.sbuf_tensor("stg2", [128, 2048], F32) as stg2,
            ):
                for wsrc, wdst in ((wkup, wkb), (wvup, wvb)):
                    p.dma(nc.sync,
                          stg2.ap().rearrange("p (a b) -> p a b", a=4),
                          wsrc[:, :, :].rearrange("a p b -> p a b"))
                    p.barrier()
                    nc.vector.tensor_copy(
                        out=wdst[:, :, :],
                        in_=stg2.ap().rearrange("p (a b) -> p a b", a=4))
                    p.barrier()
                with nc.psum_tensor(_nm("kps"), [128, 4, 512], F32) as ps:
                    for oc in range(4):
                        for tt in range(8):
                            cols = slice(tt * 512, (tt + 1) * 512)
                            for c in range(4):
                                nc.tensor.matmul(ps[:, tt % 4],
                                                 wkb[:, c, oc * 128:(oc + 1) * 128],
                                                 latT[:, c, cols],
                                                 start=(c == 0), stop=(c == 3),
                                                 skip_group_check=True)
                            p.barrier()
                            nc.vector.tensor_copy(out=kT[:, oc, cols], in_=ps[:, tt % 4])
                            p.barrier()
                with nc.psum_tensor(_nm("vps"), [128, 512], F32) as ps:
                    for tk in range(NT):
                        cols = slice(tk * 128, (tk + 1) * 128)
                        for c in range(4):
                            nc.tensor.matmul(ps[:, :], latT[:, c, cols], wvb[:, c, :],
                                             start=(c == 0), stop=(c == 3),
                                             skip_group_check=True)
                        p.barrier()
                        nc.vector.tensor_copy(
                            out=vh[:, tk, :, 0:64],
                            in_=ps.ap().rearrange("p (a b) -> p a b", a=8))
                        p.barrier()
        if phases < 3:
            _dump_qk(nc, p, yp, qT, kT)
            return nc
        with (
            nc.sbuf_tensor("cosb", [128, T], BF16) as cosb,
            nc.sbuf_tensor("sinb", [128, T], BF16) as sinb,
            nc.sbuf_tensor("stg3", [128, T], F32) as stg3,
            nc.sbuf_tensor("rtmp", [64, 4, 2048], BF16) as rtmp,
        ):
            p.dma(nc.sync, stg3[:, :], rc2[:, :])
            p.barrier()
            nc.vector.tensor_copy(out=cosb[:, :], in_=stg3[:, :])
            p.barrier()
            p.dma(nc.sync, stg3[:, :], rs2[:, :])
            p.barrier()
            nc.vector.tensor_copy(out=sinb[:, :], in_=stg3[:, :])
            p.barrier()
            for buf in (qT, kT):
                for oc in range(4):
                    for th in range(2):
                        cols = slice(th * 2048, (th + 1) * 2048)
                        for hl in range(2):
                            E = buf[64 * hl:64 * hl + 32, oc, cols]
                            O = buf[64 * hl + 32:64 * hl + 64, oc, cols]
                            cE = cosb[64 * hl:64 * hl + 32, cols]
                            cO = cosb[64 * hl + 32:64 * hl + 64, cols]
                            sE = sinb[64 * hl:64 * hl + 32, cols]
                            sO = sinb[64 * hl + 32:64 * hl + 64, cols]
                            nc.vector.tensor_mul(rtmp[0:32, 0], E, cE)
                            nc.vector.tensor_mul(rtmp[0:32, 1], O, sO)
                            nc.vector.tensor_mul(rtmp[0:32, 2], E, sE)
                            nc.vector.tensor_mul(rtmp[0:32, 3], O, cO)
                            p.barrier()
                            nc.vector.tensor_sub(E, rtmp[0:32, 0], rtmp[0:32, 1])
                            nc.vector.tensor_add(O, rtmp[0:32, 2], rtmp[0:32, 3])
                            p.barrier()
        if phases < 4:
            _dump_qk(nc, p, yp, qT, kT)
            return nc
        with (
            nc.sbuf_tensor("sq4", [128, T], BF16) as sq4,
            nc.sbuf_tensor("srow", [1, T], F32) as srow,
            nc.sbuf_tensor("sml", [1, 8], F32) as sml,
            nc.sbuf_tensor("smlb", [1, 1], BF16) as smlb,
        ):
          with nc.psum_tensor(_nm("nps"), [1, T], F32) as nps:
              for idx, buf in ((0, qT), (1, kT)):
                  for oc in range(4):
                      nc.scalar.activation(out=sq4[:, :], in_=buf[:, oc], func=AF.Square)
                      p.barrier()
                      for j in range(T // 512):
                          cols = slice(j * 512, (j + 1) * 512)
                          nc.tensor.matmul(nps[0:1, cols], onek[:, 0:1],
                                           sq4[:, cols], start=(oc == 0), stop=(oc == 3),
                                           skip_group_check=True)
                      p.barrier()
                  nc.vector.reduce_max(sml[0:1, idx:idx + 1], nps[0:1, :],
                                       axis=mybir.AxisListType.X)
                  p.barrier()
              nc.scalar.activation(out=sml[0:1, 2:4], in_=sml[0:1, 0:2], func=AF.Sqrt)
              p.barrier()
              nc.vector.tensor_mul(sml[0:1, 4:5], sml[0:1, 2:3], sml[0:1, 3:4])
              p.barrier()
              nc.vector.tensor_scalar(out=sml[0:1, 5:6], in0=sml[0:1, 4:5],
                                      scalar1=-0.125, scalar2=60.0,
                                      op0=ALU.mult, op1=ALU.add)
              p.barrier()
          nc.vector.tensor_copy(out=smlb[0:1, :], in_=sml[0:1, 5:6])
          p.barrier()
          with nc.psum_tensor(_nm("nbp"), [128, 1], F32) as nbp:
              nc.tensor.matmul(nbp[:, :], onec[0:1, :], smlb[0:1, :],
                               start=True, stop=True)
              p.barrier()
              nc.vector.tensor_copy(out=nb[:, :], in_=nbp[:, :])
              p.barrier()

        if phases < 5:
            _dump_qk(nc, p, yp, qT, kT)
            return nc
        sem_pe = nc.alloc_semaphore("att_pe")
        sem_act = nc.alloc_semaphore("att_act")
        cnt_pe = 0
        cnt_act = 0
        with (
            nc.sbuf_tensor("ebuf", [128, 2, 32, 512], BF16) as ebuf,
            nc.sbuf_tensor("r512", [1, 512], F32) as r512,
            nc.sbuf_tensor("r512b", [1, 512], BF16) as r512b,
            nc.sbuf_tensor("pvs", [64, 512], F32) as pvs,
            nc.sbuf_tensor("one64f", [1, 64], F32) as one64f,
        ):
            nc.vector.memset(one64f[:, :], 1.0)
            p.barrier()
            for g in range(8):
                ntf = 4 * g + 2
                gc = slice(g * 512, (g + 1) * 512)
                for pair in range(4):
                    with (
                        nc.psum_tensor(_nm("sA"), [128, 2, 512], F32) as sA,
                        nc.psum_tensor(_nm("sB"), [128, 2, 512], F32) as sB,
                    ):
                        for hloc in range(2):
                            for tkb in (ntf, ntf + 1):
                                nc.vector.memset(ebuf[:, hloc, tkb, 0:256], 0.0)
                        p.barrier()
                        import os as _os
                        serial = bool(_os.environ.get("L0ATT_SERIAL"))
                        for tk in range(ntf + 2):
                            par = tk % 2
                            tkc = slice(tk * 128, (tk + 1) * 128)
                            bnd = tk >= ntf
                            if serial:
                                p.barrier()
                            elif tk >= 2:
                                nc.tensor.wait_ge(sem_act, cnt_act + tk - 1)
                            nc.tensor.matmul(sA[:, par], kT[0:64, pair, tkc],
                                             qT[0:64, pair, gc], start=True, stop=True,
                                             skip_group_check=True)
                            nc.tensor.matmul(sB[:, par], kT[64:128, pair, tkc],
                                             qT[64:128, pair, gc], start=True, stop=True,
                                             skip_group_check=True,
                                             tile_position=(64, 0)).then_inc(sem_pe, 1)
                            if serial:
                                p.barrier()
                            else:
                                nc.scalar.wait_ge(sem_pe, cnt_pe + tk + 1)
                            ecols = slice(256, 512) if bnd else slice(0, 512)
                            nc.scalar.activation(out=ebuf[:, 0, tk, ecols],
                                                 in_=sA[:, par, ecols], func=AF.Exp,
                                                 bias=nb[:, 0:1], scale=0.125)
                            nc.scalar.activation(out=ebuf[:, 1, tk, ecols],
                                                 in_=sB[:, par, ecols], func=AF.Exp,
                                                 bias=nb[:, 0:1],
                                                 scale=0.125).then_inc(sem_act, 1)
                        cnt_pe += ntf + 2
                        cnt_act += ntf + 2
                        p.barrier()
                    for hloc in range(2):
                        h = pair * 2 + hloc
                        with (
                            nc.psum_tensor(_nm("pv"), [128, 512], F32) as pv,
                            nc.psum_tensor(_nm("rbc"), [64, 512], F32) as rbc,
                        ):
                            for tk in range(ntf + 2):
                                nc.tensor.matmul(pv[0:65, :], vh[:, tk, h],
                                                 ebuf[:, hloc, tk], start=(tk == 0),
                                                 stop=(tk == ntf + 1),
                                                 skip_group_check=True)
                            p.barrier()
                            nc.vector.reciprocal(r512[0:1, :], pv[64:65, :])
                            p.barrier()
                            nc.tensor.matmul(rbc[:, :], one64f[0:1, :], r512[0:1, :],
                                             start=True, stop=True)
                            nc.vector.tensor_copy(out=pvs[:, :], in_=pv[0:64, :])
                            p.barrier()
                            nc.vector.tensor_mul(
                                OT[64 * hloc:64 * (hloc + 1), pair, gc],
                                pvs[:, :], rbc[:, :])
                            p.barrier()
        if phases < 6:
            return nc
        with (
            nc.sbuf_tensor("wob", [128, 4, D], BF16) as wob,
            nc.sbuf_tensor("stg6", [128, 2048], F32) as stg6,
            nc.sbuf_tensor("yst", [128, 512], F32) as yst,
        ):
            for c2 in range(2):
                p.dma(nc.sync,
                      stg6.ap().rearrange("p (a b) -> p a b", a=2),
                      wop[c2 * 2:(c2 + 1) * 2].rearrange("a p b -> p a b"))
                p.barrier()
                nc.vector.tensor_copy(
                    out=wob[:, c2 * 2:(c2 + 1) * 2],
                    in_=stg6.ap().rearrange("p (a b) -> p a b", a=2))
                p.barrier()
            with nc.psum_tensor(_nm("yps"), [128, 512], F32) as ps:
                for dc in range(8):
                    for tt in range(T // 512):
                        cols = slice(tt * 512, (tt + 1) * 512)
                        for oc in range(4):
                            nc.tensor.matmul(ps[:, :],
                                             wob[:, oc, dc * 128:(dc + 1) * 128],
                                             OT[:, oc, cols], start=(oc == 0),
                                             stop=(oc == 3), skip_group_check=True)
                        p.barrier()
                        nc.vector.tensor_copy(out=yst[:, :], in_=ps[:, :])
                        p.barrier()
                        p.dma(nc.sync, yp[dc, :, cols], yst[:, :])
                        p.barrier()
    return nc


def prep_l0attn_inputs(x_fm, wq, w_down, wk_up, wv_up, wo):
    perm = []
    for h in range(16):
        perm += [64 * h + 2 * i for i in range(32)]
        perm += [64 * h + 2 * i + 1 for i in range(32)]
    perm = np.asarray(perm)
    wq_p = wq[:, perm]
    wk_p = wk_up[:, perm]
    inv = 1.0 / 10000.0 ** (np.arange(0, 64, 2, dtype=np.float32) / 64)
    ang = np.outer(np.arange(T, dtype=np.float32), inv)
    cos2 = np.ascontiguousarray(np.concatenate([np.cos(ang).T] * 4, axis=0)).astype(np.float32)
    sin2 = np.ascontiguousarray(np.concatenate([np.sin(ang).T] * 4, axis=0)).astype(np.float32)
    maps = []
    for c in range(NCORES):
        b, hh = c // 2, c % 2
        hcols = slice(512 * hh, 512 * (hh + 1))
        maps.append({
            "xin": np.ascontiguousarray(x_fm[:, b * T:(b + 1) * T]).reshape(8, 128, T),
            "wqp": np.ascontiguousarray(wq_p[:, hcols]).reshape(8, 128, 512),
            "wdn": np.ascontiguousarray(w_down).reshape(8, 128, 512),
            "wkup": np.ascontiguousarray(wk_p[:, hcols]).reshape(4, 128, 512),
            "wvup": np.ascontiguousarray(wv_up[:, 512 * hh:512 * (hh + 1)]).reshape(4, 128, 512),
            "wop": np.ascontiguousarray(wo[512 * hh:512 * (hh + 1), :]).reshape(4, 128, D),
            "rc2": cos2, "rs2": sin2,
        })
    return maps


def build_comb():
    nc = bass.Bass("TRN2", target_bir_lowering=False)
    xin = nc.dram_tensor("xin", [8, 128, S], F32, kind="ExternalInput").ap()
    ya = nc.dram_tensor("ya", [8, 128, S], F32, kind="ExternalInput").ap()
    yb = nc.dram_tensor("yb", [8, 128, S], F32, kind="ExternalInput").ap()
    xout = nc.dram_tensor("xout", [8, 128, S], F32, kind="ExternalOutput").ap()
    p = Prog(nc)
    import contextlib
    es = contextlib.ExitStack()
    with es:
        ysum = es.enter_context(nc.sbuf_tensor("ysum", [128, 8, S], BF16))
        stga = es.enter_context(nc.sbuf_tensor("stga", [128, S], F32))
        stgb = es.enter_context(nc.sbuf_tensor("stgb", [128, S], F32))
        sq = es.enter_context(nc.sbuf_tensor("sq", [128, S], BF16))
        rowr = es.enter_context(nc.sbuf_tensor("rowr", [1, S], F32))
        rowmr = es.enter_context(nc.sbuf_tensor("rowmr", [1, S], F32))
        rowbf = es.enter_context(nc.sbuf_tensor("rowbf", [1, S], BF16))
        for c in range(8):
            p.dma(nc.sync, stga[:, :], ya[c])
            p.dma(nc.sync, stgb[:, :], yb[c])
            p.barrier()
            nc.vector.tensor_add(ysum[:, c], stga[:, :], stgb[:, :])
            p.barrier()
        emit_ln_fm(p, None, ysum, sq, rowr[0:1], rowmr[0:1], S,
                   x_src_chunks=[ysum[:, c] for c in range(8)])
        with (
            nc.psum_tensor("rps", [128, S // 2], F32) as rps,
            nc.psum_tensor("mrps", [128, S // 2], F32) as mrps,
        ):
            for half in range(2):
                cols = slice(half * (S // 2), (half + 1) * (S // 2))
                emit_bcast_row(p, rowr[0:1, cols], rps, S // 2, rowbf)
                emit_bcast_row(p, rowmr[0:1, cols], mrps, S // 2, rowbf)
                for c in range(8):
                    p.dma(nc.sync, stga[:, 0:S // 2],
                          xin[c].rearrange("p (h t) -> p h t", h=2)[:, half])
                    p.barrier()
                    nc.vector.tensor_mul(stgb[:, 0:S // 2], ysum[:, c, cols], rps[:, :])
                    p.barrier()
                    nc.vector.tensor_sub(stgb[:, 0:S // 2], stgb[:, 0:S // 2], mrps[:, :])
                    p.barrier()
                    nc.vector.tensor_add(stgb[:, 0:S // 2], stgb[:, 0:S // 2],
                                         stga[:, 0:S // 2])
                    p.barrier()
                    p.dma(nc.sync,
                          xout[c].rearrange("p (h t) -> p h t", h=2)[:, half],
                          stgb[:, 0:S // 2])
                    p.barrier()
    return nc


def prep_comb_inputs(x_fm, ya_fm, yb_fm):
    maps = []
    for c in range(NCORES):
        sl = slice(c * S, (c + 1) * S)
        maps.append({
            "xin": np.ascontiguousarray(x_fm[:, sl]).reshape(8, 128, S),
            "ya": np.ascontiguousarray(ya_fm[:, sl]).reshape(8, 128, S),
            "yb": np.ascontiguousarray(yb_fm[:, sl]).reshape(8, 128, S),
        })
    return maps


def build_l0fattn():
    NT = T // 128
    nc = bass.Bass("TRN2", target_bir_lowering=False)
    xin = nc.dram_tensor("xin", [8, 128, T], F32, kind="ExternalInput").ap()
    wqs = nc.dram_tensor("wqs", [8, 128, 512], F32, kind="ExternalInput").ap()
    wks = nc.dram_tensor("wks", [8, 128, 512], F32, kind="ExternalInput").ap()
    wvs = nc.dram_tensor("wvs", [8, 128, 512], F32, kind="ExternalInput").ap()
    bq = nc.dram_tensor("bq", [128, 4], F32, kind="ExternalInput").ap()
    bk = nc.dram_tensor("bk", [128, 4], F32, kind="ExternalInput").ap()
    bv = nc.dram_tensor("bv", [1, 512], F32, kind="ExternalInput").ap()
    l1g = nc.dram_tensor("l1g", [128, 8], F32, kind="ExternalInput").ap()
    l1b = nc.dram_tensor("l1b", [128, 8], F32, kind="ExternalInput").ap()
    wop = nc.dram_tensor("wop", [4, 128, D], F32, kind="ExternalInput").ap()
    yp = nc.dram_tensor("yp", [8, 128, T], F32, kind="ExternalOutput").ap()

    p = Prog(nc)
    import contextlib
    es = contextlib.ExitStack()
    with es:
        qT = es.enter_context(nc.sbuf_tensor("qT", [128, 4, T], BF16))
        kT = es.enter_context(nc.sbuf_tensor("kT", [128, 4, T], BF16))
        vh = es.enter_context(nc.sbuf_tensor("vh", [128, NT, 8, 65], BF16))
        OT = es.enter_context(nc.sbuf_tensor("OT", [128, 4, T], BF16))
        nb = es.enter_context(nc.sbuf_tensor("nb", [128, 1], BF16))
        onec = es.enter_context(nc.sbuf_tensor("onec", [1, 128], BF16))
        onek = es.enter_context(nc.sbuf_tensor("onek", [128, 1], BF16))
        gbg = es.enter_context(nc.sbuf_tensor("gbg", [128, 8], F32))
        gbb = es.enter_context(nc.sbuf_tensor("gbb", [128, 8], F32))
        bqr = es.enter_context(nc.sbuf_tensor("bqr", [128, 4], F32))
        bkr = es.enter_context(nc.sbuf_tensor("bkr", [128, 4], F32))
        bvr = es.enter_context(nc.sbuf_tensor("bvr", [1, 512], BF16))
        nc.vector.memset(onec[:, :], 1.0)
        nc.vector.memset(onek[:, :], 1.0)
        nc.vector.memset(vh.ap().rearrange("p a b c -> p (a b c)")[:, :], 0.0)
        nc.vector.memset(
            vh.ap().rearrange("p a b c -> p (a b) c", c=65)[:, :, 64:65], 1.0)
        p.barrier()
        p.dma(nc.sync, gbg[:, :], l1g[:, :])
        p.dma(nc.sync, gbb[:, :], l1b[:, :])
        p.dma(nc.sync, bqr[:, :], bq[:, :])
        p.dma(nc.sync, bkr[:, :], bk[:, :])
        p.barrier()
        with nc.sbuf_tensor("bvf", [1, 512], F32) as bvf:
            p.dma(nc.sync, bvf[0:1, :], bv[:, :])
            p.barrier()
            nc.vector.tensor_copy(out=bvr[0:1, :], in_=bvf[0:1, :])
            p.barrier()

        for th2 in range(2):
          hcols = slice(th2 * 2048, (th2 + 1) * 2048)
          with (
            nc.sbuf_tensor(_nm("h1"), [128, 8, 2048], BF16) as h1,
            nc.sbuf_tensor(_nm("wstg"), [128, 8, 512], BF16) as wstg,
            nc.sbuf_tensor(_nm("stgf"), [128, 2048], F32) as stgf,
              nc.sbuf_tensor(_nm("sqb"), [128, 512], BF16) as sqb,
              nc.sbuf_tensor(_nm("rowr"), [1, 512], F32) as rowr,
              nc.sbuf_tensor(_nm("rowmr"), [1, 512], F32) as rowmr,
              nc.sbuf_tensor(_nm("rowbf"), [1, 512], BF16) as rowbf,
          ):
              for tq8 in range(th2 * 4, th2 * 4 + 4):
                  tcols = slice(tq8 * 512, (tq8 + 1) * 512)
                  with nc.sbuf_tensor(_nm("xpq"), [128, 8, 512], BF16) as xp:
                      for c in range(8):
                          p.dma(nc.sync, stgf[:, 0:512], xin[c, :, tcols])
                          p.barrier()
                          nc.vector.tensor_copy(out=xp[:, c], in_=stgf[:, 0:512])
                          p.barrier()
                      for rnd in range(2):
                          emit_ln_fm(p, None, xp, sqb, rowr[0:1], rowmr[0:1], 512,
                                     x_src_chunks=[xp[:, c] for c in range(8)])
                          with (
                              nc.psum_tensor(_nm("rps"), [128, 512], F32) as rps,
                              nc.psum_tensor(_nm("mrps"), [128, 512], F32) as mrps,
                          ):
                              emit_bcast_row(p, rowr[0:1, 0:512], rps, 512, rowbf)
                              emit_bcast_row(p, rowmr[0:1, 0:512], mrps, 512, rowbf)
                              for c in range(8):
                                  nc.vector.tensor_mul(xp[:, c], xp[:, c], rps[:, :])
                              p.barrier()
                              for c in range(8):
                                  nc.vector.tensor_sub(xp[:, c], xp[:, c], mrps[:, :])
                              p.barrier()
                              if rnd == 1:
                                  for c in range(8):
                                      nc.vector.tensor_scalar(
                                          out=h1[:, c, tcols.start - th2 * 2048:tcols.stop - th2 * 2048], in0=xp[:, c],
                                          scalar1=gbg[:, c:c + 1], scalar2=gbb[:, c:c + 1],
                                          op0=ALU.mult, op1=ALU.add)
                                  p.barrier()
              for wsrc, dst, bcol in ((wqs, qT, 0), (wks, kT, 1)):
                  for c2 in range(2):
                      p.dma(nc.sync,
                            stgf.ap().rearrange("p (a b) -> p a b", a=4)[:, :, 0:512],
                            wsrc[c2 * 4:(c2 + 1) * 4].rearrange("a p b -> p a b"))
                      p.barrier()
                      nc.vector.tensor_copy(
                          out=wstg[:, c2 * 4:(c2 + 1) * 4],
                          in_=stgf.ap().rearrange("p (a b) -> p a b", a=4)[:, :, 0:512])
                      p.barrier()
                  with nc.psum_tensor(_nm("qps"), [128, 512], F32) as ps:
                      for oc in range(4):
                          for tt in range(th2 * 4, th2 * 4 + 4):
                              cols = slice(tt * 512, (tt + 1) * 512)
                              for c in range(8):
                                  nc.tensor.matmul(ps[:, :],
                                                   wstg[:, c, oc * 128:(oc + 1) * 128],
                                                   h1[:, c, cols.start - th2 * 2048:cols.stop - th2 * 2048],
                                                   start=(c == 0), stop=(c == 7),
                                                   skip_group_check=True)
                              p.barrier()
                              nc.vector.tensor_scalar_add(
                                  dst[:, oc, cols], ps[:, :],
                                  (bqr if bcol == 0 else bkr)[:, oc:oc + 1])
                              p.barrier()
              for c2 in range(2):
                  p.dma(nc.sync,
                        stgf.ap().rearrange("p (a b) -> p a b", a=4)[:, :, 0:512],
                        wvs[c2 * 4:(c2 + 1) * 4].rearrange("a p b -> p a b"))
                  p.barrier()
                  nc.vector.tensor_copy(
                      out=wstg[:, c2 * 4:(c2 + 1) * 4],
                      in_=stgf.ap().rearrange("p (a b) -> p a b", a=4)[:, :, 0:512])
                  p.barrier()
              with nc.psum_tensor(_nm("vps"), [128, 512], F32) as ps:
                  for tk in range(th2 * 16, th2 * 16 + 16):
                      cols = slice(tk * 128 - th2 * 2048, (tk + 1) * 128 - th2 * 2048)
                      for c in range(8):
                          nc.tensor.matmul(ps[:, :], h1[:, c, cols], wstg[:, c, :],
                                           start=(c == 0), stop=False,
                                           skip_group_check=True)
                      nc.tensor.matmul(ps[:, :], onec[0:1, :], bvr[0:1, :],
                                       start=False, stop=True, skip_group_check=True)
                      p.barrier()
                      nc.vector.tensor_copy(
                          out=vh[:, tk, :, 0:64],
                          in_=ps.ap().rearrange("p (a b) -> p a b", a=8))
                      p.barrier()
        with (
            nc.sbuf_tensor("sq4", [128, T], BF16) as sq4,
            nc.sbuf_tensor("sml", [1, 8], F32) as sml,
            nc.sbuf_tensor("smlb", [1, 1], BF16) as smlb,
        ):
            with nc.psum_tensor(_nm("nps"), [1, T], F32) as nps:
                for idx, buf in ((0, qT), (1, kT)):
                    for oc in range(4):
                        nc.scalar.activation(out=sq4[:, :], in_=buf[:, oc], func=AF.Square)
                        p.barrier()
                        for j in range(T // 512):
                            cols = slice(j * 512, (j + 1) * 512)
                            nc.tensor.matmul(nps[0:1, cols], onek[:, 0:1],
                                             sq4[:, cols], start=(oc == 0), stop=(oc == 3),
                                             skip_group_check=True)
                        p.barrier()
                    nc.vector.reduce_max(sml[0:1, idx:idx + 1], nps[0:1, :],
                                         axis=mybir.AxisListType.X)
                    p.barrier()
                nc.scalar.activation(out=sml[0:1, 2:4], in_=sml[0:1, 0:2], func=AF.Sqrt)
                p.barrier()
                nc.vector.tensor_mul(sml[0:1, 4:5], sml[0:1, 2:3], sml[0:1, 3:4])
                p.barrier()
                nc.vector.tensor_scalar(out=sml[0:1, 5:6], in0=sml[0:1, 4:5],
                                        scalar1=-0.125, scalar2=60.0,
                                        op0=ALU.mult, op1=ALU.add)
                p.barrier()
            nc.vector.tensor_copy(out=smlb[0:1, :], in_=sml[0:1, 5:6])
            p.barrier()
            with nc.psum_tensor(_nm("nbp"), [128, 1], F32) as nbp:
                nc.tensor.matmul(nbp[:, :], onec[0:1, :], smlb[0:1, :],
                                 start=True, stop=True)
                p.barrier()
                nc.vector.tensor_copy(out=nb[:, :], in_=nbp[:, :])
                p.barrier()
        sem_pe = nc.alloc_semaphore("fat_pe")
        sem_act = nc.alloc_semaphore("fat_act")
        cnt_pe = 0
        cnt_act = 0
        with (
            nc.sbuf_tensor("ebuf", [128, 2, NT, 512], BF16) as ebuf,
            nc.sbuf_tensor("r512", [1, 512], F32) as r512,
            nc.sbuf_tensor("pvs", [64, 512], F32) as pvs,
            nc.sbuf_tensor("one64f", [1, 64], F32) as one64f,
        ):
            nc.vector.memset(one64f[:, :], 1.0)
            p.barrier()
            for g in range(8):
                gc = slice(g * 512, (g + 1) * 512)
                for pair in range(4):
                    with (
                        nc.psum_tensor(_nm("sA"), [128, 2, 512], F32) as sA,
                        nc.psum_tensor(_nm("sB"), [128, 2, 512], F32) as sB,
                    ):
                        for tk in range(NT):
                            par = tk % 2
                            tkc = slice(tk * 128, (tk + 1) * 128)
                            if tk >= 2:
                                nc.tensor.wait_ge(sem_act, cnt_act + tk - 1)
                            nc.tensor.matmul(sA[:, par], kT[0:64, pair, tkc],
                                             qT[0:64, pair, gc], start=True, stop=True,
                                             skip_group_check=True)
                            nc.tensor.matmul(sB[:, par], kT[64:128, pair, tkc],
                                             qT[64:128, pair, gc], start=True, stop=True,
                                             skip_group_check=True,
                                             tile_position=(64, 0)).then_inc(sem_pe, 1)
                            nc.scalar.wait_ge(sem_pe, cnt_pe + tk + 1)
                            nc.scalar.activation(out=ebuf[:, 0, tk], in_=sA[:, par],
                                                 func=AF.Exp, bias=nb[:, 0:1], scale=0.125)
                            nc.scalar.activation(out=ebuf[:, 1, tk], in_=sB[:, par],
                                                 func=AF.Exp, bias=nb[:, 0:1],
                                                 scale=0.125).then_inc(sem_act, 1)
                        cnt_pe += NT
                        cnt_act += NT
                        p.barrier()
                    for hloc in range(2):
                        h = pair * 2 + hloc
                        with (
                            nc.psum_tensor(_nm("pv"), [128, 512], F32) as pv,
                            nc.psum_tensor(_nm("rbc"), [64, 512], F32) as rbc,
                        ):
                            for tk in range(NT):
                                nc.tensor.matmul(pv[0:65, :], vh[:, tk, h],
                                                 ebuf[:, hloc, tk], start=(tk == 0),
                                                 stop=(tk == NT - 1),
                                                 skip_group_check=True)
                            p.barrier()
                            nc.vector.reciprocal(r512[0:1, :], pv[64:65, :])
                            p.barrier()
                            nc.tensor.matmul(rbc[:, :], one64f[0:1, :], r512[0:1, :],
                                             start=True, stop=True)
                            nc.vector.tensor_copy(out=pvs[:, :], in_=pv[0:64, :])
                            p.barrier()
                            nc.vector.tensor_mul(
                                OT[64 * hloc:64 * (hloc + 1), pair, gc],
                                pvs[:, :], rbc[:, :])
                            p.barrier()
        with (
            nc.sbuf_tensor("wob", [128, 4, D], BF16) as wob,
            nc.sbuf_tensor("stg6", [128, 2048], F32) as stg6,
            nc.sbuf_tensor("yst", [128, 512], F32) as yst,
        ):
            for c2 in range(2):
                p.dma(nc.sync,
                      stg6.ap().rearrange("p (a b) -> p a b", a=2),
                      wop[c2 * 2:(c2 + 1) * 2].rearrange("a p b -> p a b"))
                p.barrier()
                nc.vector.tensor_copy(
                    out=wob[:, c2 * 2:(c2 + 1) * 2],
                    in_=stg6.ap().rearrange("p (a b) -> p a b", a=2))
                p.barrier()
            with nc.psum_tensor(_nm("yps"), [128, 512], F32) as ps:
                for dc in range(8):
                    for tt in range(T // 512):
                        cols = slice(tt * 512, (tt + 1) * 512)
                        for oc in range(4):
                            nc.tensor.matmul(ps[:, :],
                                             wob[:, oc, dc * 128:(dc + 1) * 128],
                                             OT[:, oc, cols], start=(oc == 0),
                                             stop=(oc == 3), skip_group_check=True)
                        p.barrier()
                        nc.vector.tensor_copy(out=yst[:, :], in_=ps[:, :])
                        p.barrier()
                        p.dma(nc.sync, yp[dc, :, cols], yst[:, :])
                        p.barrier()
    return nc


def prep_l0fattn_inputs(x1_fm, w_qkv, b_qkv, ln1_g, ln1_b, w_o):
    maps = []
    for c in range(NCORES):
        b, hh = c // 2, c % 2
        qcols = slice(512 * hh, 512 * (hh + 1))
        kcols = slice(1024 + 512 * hh, 1024 + 512 * (hh + 1))
        vcols = slice(2048 + 512 * hh, 2048 + 512 * (hh + 1))
        maps.append({
            "xin": np.ascontiguousarray(x1_fm[:, b * T:(b + 1) * T]).reshape(8, 128, T),
            "wqs": np.ascontiguousarray(w_qkv[:, qcols]).reshape(8, 128, 512),
            "wks": np.ascontiguousarray(w_qkv[:, kcols]).reshape(8, 128, 512),
            "wvs": np.ascontiguousarray(w_qkv[:, vcols]).reshape(8, 128, 512),
            "bq": np.ascontiguousarray(b_qkv[qcols].reshape(4, 128).T),
            "bk": np.ascontiguousarray(b_qkv[kcols].reshape(4, 128).T),
            "bv": np.ascontiguousarray(b_qkv[vcols]).reshape(1, 512),
            "l1g": np.ascontiguousarray(ln1_g.reshape(8, 128).T),
            "l1b": np.ascontiguousarray(ln1_b.reshape(8, 128).T),
            "wop": np.ascontiguousarray(w_o[512 * hh:512 * (hh + 1), :]).reshape(4, 128, D),
        })
    return maps


def build_wcast():
    nc = bass.Bass("TRN2", target_bir_lowering=False)
    w1 = nc.dram_tensor("w1", [8, 128, 4096], F32, kind="ExternalInput").ap()
    w2 = nc.dram_tensor("w2", [32, 128, 1024], F32, kind="ExternalInput").ap()
    w1o = nc.dram_tensor("w1o", [8, 128, 4096], BF16, kind="ExternalOutput").ap()
    w2o = nc.dram_tensor("w2o", [32, 128, 1024], BF16, kind="ExternalOutput").ap()
    p = Prog(nc)
    with (
        nc.sbuf_tensor("st", [128, 4096], F32) as st,
        nc.sbuf_tensor("bt", [128, 4096], BF16) as bt,
    ):
        for c in range(8):
            p.dma(nc.sync, st[:, :], w1[c])
            p.barrier()
            nc.vector.tensor_copy(out=bt[:, :], in_=st[:, :])
            p.barrier()
            p.dma(nc.sync, w1o[c], bt[:, :])
            p.barrier()
        for c in range(32):
            p.dma(nc.sync, st[:, 0:1024], w2[c])
            p.barrier()
            nc.vector.tensor_copy(out=bt[:, 0:1024], in_=st[:, 0:1024])
            p.barrier()
            p.dma(nc.sync, w2o[c], bt[:, 0:1024])
            p.barrier()
    return nc


def build_l0fmoe():
    NE, HID, NH = 8, 4096, 32
    nc = bass.Bass("TRN2", target_bir_lowering=False)
    xin = nc.dram_tensor("xin", [8, 128, S], F32, kind="ExternalInput").ap()
    ya = nc.dram_tensor("ya", [8, 128, S], F32, kind="ExternalInput").ap()
    yb = nc.dram_tensor("yb", [8, 128, S], F32, kind="ExternalInput").ap()
    bo = nc.dram_tensor("bo", [128, 8], F32, kind="ExternalInput").ap()
    l2g = nc.dram_tensor("l2g", [128, 8], F32, kind="ExternalInput").ap()
    l2b = nc.dram_tensor("l2b", [128, 8], F32, kind="ExternalInput").ap()
    gw = nc.dram_tensor("gw", [8, 128, 8], F32, kind="ExternalInput").ap()
    gbr = nc.dram_tensor("gbr", [1, 8], F32, kind="ExternalInput").ap()
    w1b = nc.dram_tensor("w1b", [NE, 8, 128, HID], BF16, kind="ExternalInput").ap()
    w2b = nc.dram_tensor("w2b", [NE, NH, 128, D], BF16, kind="ExternalInput").ap()
    b1c = nc.dram_tensor("b1c", [NE, 128, NH], F32, kind="ExternalInput").ap()
    b2r = nc.dram_tensor("b2r", [NE, 1, D], F32, kind="ExternalInput").ap()
    ident = nc.dram_tensor("ident", [128, 128], F32, kind="ExternalInput").ap()
    xout = nc.dram_tensor("xout", [8, 128, S], F32, kind="ExternalOutput").ap()

    p = Prog(nc)
    import contextlib
    es = contextlib.ExitStack()
    with es:
        xa1 = es.enter_context(nc.sbuf_tensor("xa1", [128, 8, S], BF16))
        h2 = es.enter_context(nc.sbuf_tensor("h2", [128, 8, S], BF16))
        ybuf = es.enter_context(nc.sbuf_tensor("ybuf", [128, 8, S], BF16))
        ew = es.enter_context(nc.sbuf_tensor("ew", [128, 16, 8], F32))
        bor = es.enter_context(nc.sbuf_tensor("bor", [128, 8], F32))
        g2r = es.enter_context(nc.sbuf_tensor("g2r", [128, 8], F32))
        b2rr = es.enter_context(nc.sbuf_tensor("b2rr", [128, 8], F32))
        onec = es.enter_context(nc.sbuf_tensor("onec", [1, 128], BF16))
        idb = es.enter_context(nc.sbuf_tensor("idb", [128, 128], BF16))
        stgf = es.enter_context(nc.sbuf_tensor("stgf", [128, 1024], F32))
        stgg = es.enter_context(nc.sbuf_tensor("stgg", [128, 512], F32))
        sqb = es.enter_context(nc.sbuf_tensor("sqb", [128, 512], BF16))
        rowr = es.enter_context(nc.sbuf_tensor("rowr", [1, 512], F32))
        rowmr = es.enter_context(nc.sbuf_tensor("rowmr", [1, 512], F32))
        rowbf = es.enter_context(nc.sbuf_tensor("rowbf", [1, 512], BF16))
        nc.vector.memset(onec[:, :], 1.0)
        p.dma(nc.sync, bor[:, :], bo[:, :])
        p.dma(nc.sync, g2r[:, :], l2g[:, :])
        p.dma(nc.sync, b2rr[:, :], l2b[:, :])
        p.dma(nc.sync, stgf[:, 0:128], ident[:, :])
        p.barrier()
        nc.vector.tensor_copy(out=idb[:, :], in_=stgf[:, 0:128])
        p.barrier()

        for tq in range(S // 512):
            tcols = slice(tq * 512, (tq + 1) * 512)
            with nc.sbuf_tensor(_nm("xq"), [128, 8, 512], BF16) as xq:
                for c in range(8):
                    p.dma(nc.sync, stgg[:, :], xin[c, :, tcols])
                    p.barrier()
                    nc.vector.tensor_copy(out=xq[:, c], in_=stgg[:, :])
                    p.barrier()
                emit_ln_fm(p, None, xq, sqb, rowr[0:1], rowmr[0:1], 512,
                           x_src_chunks=[xq[:, c] for c in range(8)])
                with (
                    nc.psum_tensor(_nm("rps"), [128, 512], F32) as rps,
                    nc.psum_tensor(_nm("mrps"), [128, 512], F32) as mrps,
                ):
                    emit_bcast_row(p, rowr[0:1, 0:512], rps, 512, rowbf)
                    emit_bcast_row(p, rowmr[0:1, 0:512], mrps, 512, rowbf)
                    for c in range(8):
                        nc.vector.tensor_mul(xq[:, c], xq[:, c], rps[:, :])
                    p.barrier()
                    for c in range(8):
                        nc.vector.tensor_sub(xq[:, c], xq[:, c], mrps[:, :])
                    p.barrier()
                for c in range(8):
                    p.dma(nc.sync, stgg[:, :], ya[c, :, tcols])
                    p.dma(nc.sync, stgf[:, 0:512], yb[c, :, tcols])
                    p.barrier()
                    nc.vector.tensor_add(stgg[:, :], stgg[:, :], stgf[:, 0:512])
                    p.barrier()
                    nc.vector.tensor_scalar_add(stgg[:, :], stgg[:, :], bor[:, c:c + 1])
                    p.barrier()
                    nc.vector.tensor_add(xa1[:, c, tcols], xq[:, c], stgg[:, :])
                    p.barrier()
            with nc.sbuf_tensor(_nm("hq"), [128, 8, 512], BF16) as hq:
                for c in range(8):
                    nc.vector.tensor_copy(out=hq[:, c], in_=xa1[:, c, tcols])
                p.barrier()
                emit_ln_fm(p, None, hq, sqb, rowr[0:1], rowmr[0:1], 512,
                           x_src_chunks=[hq[:, c] for c in range(8)])
                with (
                    nc.psum_tensor(_nm("rps"), [128, 512], F32) as rps,
                    nc.psum_tensor(_nm("mrps"), [128, 512], F32) as mrps,
                ):
                    emit_bcast_row(p, rowr[0:1, 0:512], rps, 512, rowbf)
                    emit_bcast_row(p, rowmr[0:1, 0:512], mrps, 512, rowbf)
                    for c in range(8):
                        nc.vector.tensor_mul(hq[:, c], hq[:, c], rps[:, :])
                    p.barrier()
                    for c in range(8):
                        nc.vector.tensor_sub(hq[:, c], hq[:, c], mrps[:, :])
                    p.barrier()
                    for c in range(8):
                        nc.vector.tensor_scalar(out=h2[:, c, tcols], in0=hq[:, c],
                                                scalar1=g2r[:, c:c + 1],
                                                scalar2=b2rr[:, c:c + 1],
                                                op0=ALU.mult, op1=ALU.add)
                    p.barrier()

        with (
            nc.sbuf_tensor("gwb", [128, 8, 8], BF16) as gwb,
            nc.sbuf_tensor("gbb", [1, 8], BF16) as gbb,
            nc.sbuf_tensor("lgt", [128, 8], F32) as lgt,
            nc.sbuf_tensor("m1", [128, 1], F32) as m1,
            nc.sbuf_tensor("m2", [128, 1], F32) as m2,
            nc.sbuf_tensor("eq1", [128, 8], F32) as eq1,
            nc.sbuf_tensor("eq2", [128, 8], F32) as eq2,
            nc.sbuf_tensor("gw1", [128, 1], F32) as gw1,
            nc.sbuf_tensor("gw2", [128, 1], F32) as gw2,
        ):
            for c in range(8):
                p.dma(nc.sync, stgf[:, 0:8], gw[c])
                p.barrier()
                nc.vector.tensor_copy(out=gwb[:, c], in_=stgf[:, 0:8])
                p.barrier()
            p.dma(nc.sync, stgf[0:1, 0:8], gbr[:, :])
            p.barrier()
            nc.vector.tensor_copy(out=gbb[0:1, :], in_=stgf[0:1, 0:8])
            p.barrier()
            with nc.psum_tensor(_nm("lps"), [128, 8], F32) as lps:
                for tt in range(16):
                    ts_ = slice(tt * 128, (tt + 1) * 128)
                    for c in range(8):
                        nc.tensor.matmul(lps[:, :], h2[:, c, ts_], gwb[:, c],
                                         start=(c == 0), stop=False,
                                         skip_group_check=True)
                    nc.tensor.matmul(lps[:, :], onec[0:1, :], gbb[0:1, :],
                                     start=False, stop=True, skip_group_check=True)
                    p.barrier()
                    nc.vector.tensor_copy(out=lgt[:, :], in_=lps[:, :])
                    p.barrier()
                    nc.vector.reduce_max(m1[:, 0:1], lgt[:, :], axis=mybir.AxisListType.X)
                    p.barrier()
                    nc.vector.tensor_scalar(out=eq1[:, :], in0=lgt[:, :],
                                            scalar1=m1[:, 0:1], scalar2=None,
                                            op0=ALU.is_equal)
                    p.barrier()
                    nc.vector.scalar_tensor_tensor(out=eq2[:, :], in0=eq1[:, :],
                                                   scalar=-1e30, in1=lgt[:, :],
                                                   op0=ALU.mult, op1=ALU.add)
                    p.barrier()
                    nc.vector.reduce_max(m2[:, 0:1], eq2[:, :], axis=mybir.AxisListType.X)
                    p.barrier()
                    nc.vector.tensor_sub(gw1[:, 0:1], m1[:, 0:1], m2[:, 0:1])
                    p.barrier()
                    nc.scalar.activation(out=gw1[:, 0:1], in_=gw1[:, 0:1],
                                         func=AF.Sigmoid)
                    p.barrier()
                    nc.vector.tensor_scalar(out=gw2[:, 0:1], in0=gw1[:, 0:1],
                                            scalar1=-1.0, scalar2=1.0,
                                            op0=ALU.mult, op1=ALU.add)
                    p.barrier()
                    nc.vector.tensor_scalar(out=eq2[:, :], in0=eq2[:, :],
                                            scalar1=m2[:, 0:1], scalar2=None,
                                            op0=ALU.is_equal)
                    p.barrier()
                    nc.vector.tensor_scalar_mul(eq1[:, :], eq1[:, :], gw1[:, 0:1])
                    nc.vector.tensor_scalar_mul(eq2[:, :], eq2[:, :], gw2[:, 0:1])
                    p.barrier()
                    nc.vector.tensor_add(ew[:, tt, :], eq1[:, :], eq2[:, :])
                    p.barrier()

        with (
            nc.sbuf_tensor("macc", [128, 4, D], F32) as macc,
            nc.sbuf_tensor("hidb", [128, NH, 512], BF16) as hidb,
            nc.sbuf_tensor("w1c", [128, 8, 512], BF16) as w1c,
            nc.sbuf_tensor("w2c", [128, 4, D], BF16) as w2c,
            nc.sbuf_tensor("b1s", [128, NH], F32) as b1s,
            nc.sbuf_tensor("b2s", [1, D], BF16) as b2s,
            nc.sbuf_tensor("mtb", [128, D], BF16) as mtb,
        ):
            for sup in range(S // 512):
                scols = slice(sup * 512, (sup + 1) * 512)
                nc.vector.memset(macc.ap().rearrange("p a b -> p (a b)")[:, :], 0.0)
                p.barrier()
                for e in range(NE):
                    p.dma(nc.sync, b1s[:, :], b1c[e])
                    p.dma(nc.sync, stgf[0:1, 0:1024], b2r[e])
                    p.barrier()
                    nc.vector.tensor_copy(out=b2s[0:1, :], in_=stgf[0:1, 0:1024])
                    p.barrier()
                    with nc.psum_tensor(_nm("hps"), [128, 512], F32) as hps:
                        for hg in range(NH // 4):
                            p.dma(nc.sync, w1c[:, :, :],
                                  w1b[e, :, :, hg * 512:(hg + 1) * 512]
                                  .rearrange("a p b -> p a b"))
                            p.barrier()
                            for hl in range(4):
                                hc = hg * 4 + hl
                                for c in range(8):
                                    nc.tensor.matmul(hps[:, :],
                                                     w1c[:, c, hl * 128:(hl + 1) * 128],
                                                     h2[:, c, scols],
                                                     start=(c == 0), stop=(c == 7),
                                                     skip_group_check=True)
                                p.barrier()
                                nc.scalar.activation(out=hidb[:, hc], in_=hps[:, :],
                                                     func=AF.Relu, bias=b1s[:, hc:hc + 1])
                                p.barrier()
                    with nc.psum_tensor(_nm("ops"), [128, 4, 2, 512], F32) as ops:
                        for hg in range(NH // 4):
                            p.dma(nc.sync, w2c[:, :, :], w2b[e, hg * 4:(hg + 1) * 4]
                                  .rearrange("a p b -> p a b"))
                            p.barrier()
                            for hl in range(4):
                                hc = hg * 4 + hl
                                for tl in range(4):
                                    lcols = slice(tl * 128, (tl + 1) * 128)
                                    for oc in range(2):
                                        nc.tensor.matmul(ops[:, tl, oc],
                                                         hidb[:, hc, lcols],
                                                         w2c[:, hl, oc * 512:(oc + 1) * 512],
                                                         start=(hc == 0), stop=False,
                                                         skip_group_check=True)
                            p.barrier()
                        for tl in range(4):
                            for oc in range(2):
                                nc.tensor.matmul(ops[:, tl, oc], onec[0:1, 0:128],
                                                 b2s[0:1, oc * 512:(oc + 1) * 512],
                                                 start=False, stop=True,
                                                 skip_group_check=True)
                        p.barrier()
                        for tl in range(4):
                            tg = sup * 4 + tl
                            for oc in range(2):
                                nc.vector.scalar_tensor_tensor(
                                    out=macc[:, tl, oc * 512:(oc + 1) * 512],
                                    in0=ops[:, tl, oc], scalar=ew[:, tg, e:e + 1],
                                    in1=macc[:, tl, oc * 512:(oc + 1) * 512],
                                    op0=ALU.mult, op1=ALU.add)
                        p.barrier()
                for tl in range(4):
                    nc.vector.tensor_copy(out=mtb[:, :], in_=macc[:, tl])
                    p.barrier()
                    with nc.psum_tensor(_nm("tps"), [128, 8, 128], BF16) as tps:
                        for dc in range(8):
                            nc.tensor.transpose(tps[:, dc],
                                                mtb[:, dc * 128:(dc + 1) * 128],
                                                idb[:, :])
                        p.barrier()
                        gcols = slice(sup * 512 + tl * 128, sup * 512 + (tl + 1) * 128)
                        for dc in range(8):
                            nc.vector.tensor_add(ybuf[:, dc, gcols], tps[:, dc],
                                                 xa1[:, dc, gcols])
                        p.barrier()

        with (
            nc.sbuf_tensor("fsq", [128, S], BF16) as fsq,
            nc.sbuf_tensor("fr", [1, S], F32) as fr,
            nc.sbuf_tensor("fmr", [1, S], F32) as fmr,
            nc.sbuf_tensor("frbf", [1, S], BF16) as frbf,
        ):
          emit_ln_fm(p, None, ybuf, fsq, fr[0:1], fmr[0:1], S,
                     x_src_chunks=[ybuf[:, c] for c in range(8)])
          with (
            nc.psum_tensor("rps2", [128, 512], F32) as rps2,
            nc.psum_tensor("mrps2", [128, 512], F32) as mrps2,
          ):
            for tq in range(S // 512):
                cols = slice(tq * 512, (tq + 1) * 512)
                emit_bcast_row(p, fr[0:1, cols], rps2, 512, frbf)
                emit_bcast_row(p, fmr[0:1, cols], mrps2, 512, frbf)
                for c in range(8):
                    p.dma(nc.sync, stgg[:, :], xin[c, :, cols])
                    p.barrier()
                    nc.vector.tensor_mul(stgf[:, 0:512], ybuf[:, c, cols], rps2[:, :])
                    p.barrier()
                    nc.vector.tensor_sub(stgf[:, 0:512], stgf[:, 0:512], mrps2[:, :])
                    p.barrier()
                    nc.vector.tensor_add(stgf[:, 0:512], stgf[:, 0:512], stgg[:, :])
                    p.barrier()
                    p.dma(nc.sync, xout[c, :, cols], stgf[:, 0:512])
                    p.barrier()
    return nc


def prep_l0fmoe_inputs(x1_fm, ya_fm, yb_fm, b_o, ln2_g, ln2_b, gate_w, gate_b,
                       w1b_all, w2b_all, b1, b2):
    ident = np.eye(128, dtype=np.float32)
    w1r = np.ascontiguousarray(w1b_all).reshape(8, 8, 128, 4096)
    w2r = np.ascontiguousarray(w2b_all).reshape(8, 32, 128, 1024)
    b1c = np.ascontiguousarray(b1.reshape(8, 32, 128).transpose(0, 2, 1))
    b2r = np.ascontiguousarray(b2).reshape(8, 1, 1024)
    maps = []
    for c in range(NCORES):
        sl = slice(c * S, (c + 1) * S)
        maps.append({
            "xin": np.ascontiguousarray(x1_fm[:, sl]).reshape(8, 128, S),
            "ya": np.ascontiguousarray(ya_fm[:, sl]).reshape(8, 128, S),
            "yb": np.ascontiguousarray(yb_fm[:, sl]).reshape(8, 128, S),
            "bo": np.ascontiguousarray(b_o.reshape(8, 128).T),
            "l2g": np.ascontiguousarray(ln2_g.reshape(8, 128).T),
            "l2b": np.ascontiguousarray(ln2_b.reshape(8, 128).T),
            "gw": np.ascontiguousarray(gate_w).reshape(8, 128, 8),
            "gbr": np.ascontiguousarray(gate_b).reshape(1, 8),
            "w1b": w1r, "w2b": w2r, "b1c": b1c, "b2r": b2r,
            "ident": ident,
        })
    return maps


def kernel(**inputs):
    inp = {k: np.asarray(v) for k, v in inputs.items()}
    x = inp["L0_attn_x"]
    x_fm = np.ascontiguousarray(x.reshape(NTOK, D).T)

    def gather_fm(res, key, per_core_cols):
        out = np.zeros((D, NTOK), np.float32)
        for c in range(NCORES):
            out[:, per_core_cols(c)] = res.results[c][key].reshape(D, -1)
        return out

    nc = build_l0attn()
    maps = prep_l0attn_inputs(x_fm, inp["L0_attn_wq"], inp["L0_attn_w_down"],
                              inp["L0_attn_wk_up"], inp["L0_attn_wv_up"],
                              inp["L0_attn_wo"])
    res = run_bass_kernel_spmd(nc, maps, core_ids=list(range(NCORES)))
    ya_fm = np.zeros((D, NTOK), np.float32)
    yb_fm = np.zeros((D, NTOK), np.float32)
    for b in range(B):
        ya_fm[:, b * T:(b + 1) * T] = res.results[2 * b]["yp"].reshape(D, T)
        yb_fm[:, b * T:(b + 1) * T] = res.results[2 * b + 1]["yp"].reshape(D, T)
    nc = build_comb()
    res = run_bass_kernel_spmd(nc, prep_comb_inputs(x_fm, ya_fm, yb_fm),
                               core_ids=list(range(NCORES)))
    x1_fm = np.concatenate([res.results[c]["xout"].reshape(D, S)
                            for c in range(NCORES)], axis=1)

    nc = build_wcast()
    maps = [{"w1": np.ascontiguousarray(inp["L0_ffn_W1"][c]).reshape(8, 128, 4096),
             "w2": np.ascontiguousarray(inp["L0_ffn_W2"][c]).reshape(32, 128, 1024)}
            for c in range(NCORES)]
    res = run_bass_kernel_spmd(nc, maps, core_ids=list(range(NCORES)))
    w1b = np.stack([res.results[c]["w1o"].reshape(1024, 4096) for c in range(8)])
    w2b = np.stack([res.results[c]["w2o"].reshape(4096, 1024) for c in range(8)])
    nc = build_l0fattn()
    maps = prep_l0fattn_inputs(x1_fm, inp["L0_ffn_w_qkv"], inp["L0_ffn_b_qkv"],
                               inp["L0_ffn_ln1_g"], inp["L0_ffn_ln1_b"],
                               inp["L0_ffn_w_o"])
    res = run_bass_kernel_spmd(nc, maps, core_ids=list(range(NCORES)))
    for b in range(B):
        ya_fm[:, b * T:(b + 1) * T] = res.results[2 * b]["yp"].reshape(D, T)
        yb_fm[:, b * T:(b + 1) * T] = res.results[2 * b + 1]["yp"].reshape(D, T)
    nc = build_l0fmoe()
    maps = prep_l0fmoe_inputs(x1_fm, ya_fm, yb_fm, inp["L0_ffn_b_o"],
                              inp["L0_ffn_ln2_g"], inp["L0_ffn_ln2_b"],
                              inp["L0_ffn_gate_w"], inp["L0_ffn_gate_b"],
                              w1b, w2b, inp["L0_ffn_b1"], inp["L0_ffn_b2"])
    res = run_bass_kernel_spmd(nc, maps, core_ids=list(range(NCORES)))
    x2_fm = np.concatenate([res.results[c]["xout"].reshape(D, S)
                            for c in range(NCORES)], axis=1)

    def fm_slices(fm):
        return [np.ascontiguousarray(fm[:, c * S:(c + 1) * S]).reshape(8, 128, S)
                for c in range(NCORES)]

    nc = build_premid(True, 4096)
    wT = np.ascontiguousarray(inp["L1_attn_Ws"].T).reshape(8, 128, 4096)
    maps = [{"xin": xs, "wT": wT,
             "b0": np.ascontiguousarray(inp["L1_attn_bWs"]).reshape(1, 4096),
             "b1": np.ascontiguousarray(inp["L1_attn_bUs"]).reshape(1, 4096)}
            for xs in fm_slices(x2_fm)]
    res = run_bass_kernel_spmd(nc, maps, core_ids=list(range(NCORES)))
    xw_s = np.concatenate([r["xwout"] for r in res.results], axis=0)

    nc = build_scan(T)
    zeros64 = np.zeros((64, 1024), np.float32)
    zerosP = np.zeros((1024, 64), np.float32)
    xa0 = np.zeros((B, T, 64), dtype=BF)
    maps = prep_scan_inputs(inp["L1_attn_Us"], zeros64, zerosP,
                            inp["L1_attn_alpha"], xw_s.reshape(B, T, 4096), xa0)
    res = run_bass_kernel_spmd(nc, maps, core_ids=list(range(NCORES)))
    sh = np.stack([res.results[b]["sh"] for b in range(B)])

    nc = build_premid(False, 4160)
    sh_fm = np.ascontiguousarray(sh.reshape(NTOK, 1024).T.astype(np.float32))
    wma = np.ascontiguousarray(
        np.concatenate([inp["L1_attn_Wm"].T, inp["L1_attn_A"].T], axis=1)
    ).reshape(8, 128, 4160)
    maps = [{"xin": xs, "wT": wma,
             "b0": np.ascontiguousarray(inp["L1_attn_bWm"]).reshape(1, 4096),
             "b1": np.ascontiguousarray(inp["L1_attn_bUm"]).reshape(1, 4096)}
            for xs in fm_slices(sh_fm)]
    res = run_bass_kernel_spmd(nc, maps, core_ids=list(range(NCORES)))
    xwm_all = np.concatenate([r["xwout"] for r in res.results], axis=0)

    nc = build_scan(T)
    maps = prep_scan_inputs(inp["L1_attn_Um"], inp["L1_attn_Bm"],
                            inp["L1_attn_P"], np.ones(1024, np.float32),
                            np.ascontiguousarray(xwm_all[:, 0:4096]).reshape(B, T, 4096),
                            np.ascontiguousarray(xwm_all[:, 4096:4160]).reshape(B, T, 64))
    res = run_bass_kernel_spmd(nc, maps, core_ids=list(range(NCORES)))
    hm = np.stack([res.results[b]["sh"] for b in range(B)])

    nc = build_post()
    hm_rows = hm.reshape(NTOK, 1024)
    ident = np.eye(128, dtype=np.float32)
    maps = []
    for c, xs in enumerate(fm_slices(x2_fm)):
        maps.append({"xin": xs,
                     "hm": np.ascontiguousarray(hm_rows[c * S:(c + 1) * S]),
                     "ident": ident})
    res = run_bass_kernel_spmd(nc, maps, core_ids=list(range(NCORES)))
    x3_fm = np.concatenate([res.results[c]["xout"].reshape(D, S)
                            for c in range(NCORES)], axis=1)

    nc = build_l1ffn()
    maps = prep_l1ffn_inputs(x3_fm, inp["L1_ffn_w_gate"], inp["L1_ffn_w_up"],
                             inp["L1_ffn_w_down"])
    res = run_bass_kernel_spmd(nc, maps, core_ids=list(range(NCORES)))
    out_fm = np.concatenate([res.results[c]["xout"].reshape(D, S)
                             for c in range(NCORES)], axis=1)
    return np.ascontiguousarray(out_fm.T).reshape(B, T, D)
